# version 04 (final)
# baseline (speedup 1.0000x reference)
import numpy as np
import ml_dtypes
from contextlib import ExitStack

import concourse.bass as bass
import concourse.mybir as mybir
from concourse.bass import IndirectOffsetOnAxis
from concourse.bass_utils import run_bass_kernel_spmd

F32 = mybir.dt.float32
BF16 = mybir.dt.bfloat16
I32 = mybir.dt.int32
AF = mybir.ActivationFunctionType
ALU = mybir.AluOpType
AX = mybir.AxisListType.X

NC = 8
T = 2048
D = 1024
NE = 8
NG = 8
NCH = T // 128
EPS_LN = 1e-5
EPS_AT = 1e-6
BF = ml_dtypes.bfloat16

GS = 384
NGRP = 18
NSLOT = NGRP * GS

PHASES = ("ffn1", "attn", "ffn2")


class Prog:
    def __init__(self, nc):
        self.nc = nc
        self.ctx = ExitStack()
        sem = lambda n: self.ctx.enter_context(nc.semaphore(n))
        self.sv = sem("sv")
        self.sa = sem("sa")
        self.spe = sem("spe")
        self.sdg = sem("sdg")
        self.sds = sem("sds")
        self.scc = sem("scc")
        self.vc = self.ac = self.pec = 0
        self.dgc = self.dsc = self.ccc = 0
        self.streams = {}
        for nm, kind in (("w0", "g"), ("w1", "g"), ("w2", "g"), ("w3", "g"),
                         ("x0", "s"), ("x1", "s"), ("xg0", "g"), ("xg1", "g"),
                         ("o0", "s"), ("o1", "s"), ("sc0", "g"), ("sc1", "g"),
                         ("t0", "s"), ("t1", "s")):
            self.streams[nm] = [self.ctx.enter_context(nc.semaphore("st_" + nm)),
                                kind, 0]

    def DQ(self, nm, out, in_):
        sem, kind, _ = self.streams[nm]
        eng = self.nc.gpsimd if kind == "g" else self.nc.sync
        eng.dma_start(out=out, in_=in_).then_inc(sem, 16)
        self.streams[nm][2] += 16
        return self.streams[nm][2]

    def DQI(self, nm, out, src_dram, idx):
        sem = self.streams[nm][0]
        self.nc.gpsimd.indirect_dma_start(
            out, None, src_dram, IndirectOffsetOnAxis(idx, 0),
            oob_is_err=False).then_inc(sem, 16)
        self.streams[nm][2] += 16
        return self.streams[nm][2]

    def DQS(self, nm, dst_dram, idx, src_sbuf):
        sem = self.streams[nm][0]
        self.nc.gpsimd.indirect_dma_start(
            dst_dram, IndirectOffsetOnAxis(idx, 0), src_sbuf, None,
            oob_is_err=False).then_inc(sem, 16)
        self.streams[nm][2] += 16
        return self.streams[nm][2]

    def DT(self, nm, out, in_):
        sem = self.streams[nm][0]
        self.nc.sync.dma_start(out=out, in_=in_, transpose=True).then_inc(sem, 16)
        self.streams[nm][2] += 16
        return self.streams[nm][2]

    def sq(self, nm):
        return self.streams[nm][0]

    def V(self, op, *a, **kw):
        if self.vc > 0:
            self.nc.vector.wait_ge(self.sv, self.vc)
        getattr(self.nc.vector, op)(*a, **kw).then_inc(self.sv, 1)
        self.vc += 1
        return self.vc

    def A(self, op, *a, **kw):
        if self.ac > 0:
            self.nc.scalar.wait_ge(self.sa, self.ac)
        getattr(self.nc.scalar, op)(*a, **kw).then_inc(self.sa, 1)
        self.ac += 1
        return self.ac

    def MM(self, out, lhsT, rhs, start, stop, inc=False, tp=None):
        kw = {"tile_position": tp} if tp is not None else {}
        inst = self.nc.tensor.matmul(out, lhsT, rhs, start=start, stop=stop, **kw)
        if inc:
            inst.then_inc(self.spe, 1)
            self.pec += 1
        return self.pec

    def TR(self, out_ps, in_, ident):
        self.nc.tensor.transpose(out_ps, in_, ident).then_inc(self.spe, 1)
        self.pec += 1
        return self.pec

    def DG(self, out, in_):
        self.nc.gpsimd.dma_start(out=out, in_=in_).then_inc(self.sdg, 16)
        self.dgc += 16
        return self.dgc

    def DS(self, out, in_):
        self.nc.sync.dma_start(out=out, in_=in_).then_inc(self.sds, 16)
        self.dsc += 16
        return self.dsc

    def wT(self, sem, val):
        self.nc.tensor.wait_ge(sem, val)

    def wV(self, sem, val):
        self.nc.vector.wait_ge(sem, val)

    def wA(self, sem, val):
        self.nc.scalar.wait_ge(sem, val)

    def wG(self, sem, val):
        self.nc.gpsimd.wait_ge(sem, val)

    def wS(self, sem, val):
        self.nc.sync.wait_ge(sem, val)

    def drain_barrier(self):
        self.wG(self.sdg, self.dgc)
        self.wS(self.sds, self.dsc)
        for nm, (sem, kind, cntv) in self.streams.items():
            if cntv:
                (self.wG if kind == "g" else self.wS)(sem, cntv)
        self.nc.all_engine_barrier()


def build_program(phases=PHASES, attn_stop=None):
    nc = bass.Bass("TRN2", target_bir_lowering=False)
    dt = nc.dram_tensor

    x_in = dt("x_in", [D, T], F32, kind="ExternalInput").ap()
    wq_p = dt("wq_p", [NG, 128, 8, 128], F32, kind="ExternalInput").ap()
    wk_p = dt("wk_p", [NG, 128, 8, 128], F32, kind="ExternalInput").ap()
    wv_p = dt("wv_p", [NG, 128, 8, 128], F32, kind="ExternalInput").ap()
    wo_p = dt("wo_p", [8, 128, 8, 128], F32, kind="ExternalInput").ap()
    w1_p = dt("w1_p", [NE, 32, 128, 8, 128], F32, kind="ExternalInput").ap()
    w2t_p = dt("w2t_p", [NE, 32, 128, 1024], F32, kind="ExternalInput").ap()
    ws1_p = dt("ws1_p", [32, 128, 8, 128], F32, kind="ExternalInput").ap()
    ws2_p = dt("ws2_p", [8, 128, 32, 128], F32, kind="ExternalInput").ap()
    gw_p = dt("gw_p", [128, 64], F32, kind="ExternalInput").ap()
    gbb_p = dt("gbb_p", [128, 8], F32, kind="ExternalInput").ap()
    b1r2_p = dt("b1r2_p", [NE * 128, 32], F32, kind="ExternalInput").ap()
    b2n_p = dt("b2n_p", [NE, 1024], F32, kind="ExternalInput").ap()
    triex_p = dt("triex_p", [128, 128], F32, kind="ExternalInput").ap()
    dcrow_p = dt("dcrow_p", [128, 32], F32, kind="ExternalInput").ap()
    iota128_p = dt("iota128_p", [128, 1], F32, kind="ExternalInput").ap()
    iotaj_p = dt("iotaj_p", [8, NGRP], F32, kind="ExternalInput").ap()
    bs1_p = dt("bs1_p", [128, 32], F32, kind="ExternalInput").ap()
    bs2_p = dt("bs2_p", [128, 8], F32, kind="ExternalInput").ap()
    bq_p = dt("bq_p", [128, 8], F32, kind="ExternalInput").ap()
    bk_p = dt("bk_p", [128, 8], F32, kind="ExternalInput").ap()
    bv_p = dt("bv_p", [128, 8], F32, kind="ExternalInput").ap()
    bo_p = dt("bo_p", [128, 8], F32, kind="ExternalInput").ap()
    umask_p = dt("umask_p", [128, 128], F32, kind="ExternalInput").ap()
    ident_p = dt("ident_p", [128, 128], F32, kind="ExternalInput").ap()
    ones_p = dt("ones_p", [128, 128], F32, kind="ExternalInput").ap()
    sel_p = dt("sel_p", [128, 128], F32, kind="ExternalInput").ap()
    seleps_p = dt("seleps_p", [128, 128], F32, kind="ExternalInput").ap()
    selt_p = dt("selt_p", [16, NG * 128], F32, kind="ExternalInput").ap()
    n16_p = dt("n16_p", [16, T], F32, kind="ExternalInput").ap()
    rn16_p = dt("rn16_p", [16, T], F32, kind="ExternalInput").ap()
    rmask_p = dt("rmask_p", [128, 8], F32, kind="ExternalInput").ap()

    x_out = dt("x_out", [D, T], F32, kind="ExternalOutput").ap()

    xres = dt("xres", [D, T], F32).ap()
    xmid = dt("xmid", [D, T], F32).ap()
    wc1 = dt("wc1", [NE * 32 * 128, 1024], BF16).ap()
    wc2 = dt("wc2", [NE * 32 * 128, 1024], BF16).ap()
    wcs1 = dt("wcs1", [32 * 128, 1024], BF16).ap()
    wcs2 = dt("wcs2", [8 * 128, 4096], BF16).ap()
    wcb2 = dt("wcb2", [NE, 1024], BF16).ap()
    estage = dt("estage", [NSLOT, 1024], BF16).ap()
    ystage = dt("ystage", [NSLOT, 1024], BF16).ap()
    xnstage = dt("xnstage", [D, T], BF16).ap()
    ybuf = dt("ybuf", [T, D], BF16).ap()
    cc_in = [dt(f"cc{i}_in", [128, 16], F32) for i in range(2)]
    cc_out = [dt(f"cc{i}_out", [128 * NC, 16], F32) for i in range(2)]
    cc3_in = dt("cc3_in", [16, 1], F32)
    cc3_out = dt("cc3_out", [16 * NC, 1], F32)
    cc4_in = dt("cc4_in", [128, NG * 64], F32)
    cc4_out = dt("cc4_out", [128 * NC, NG * 64], F32)

    p = Prog(nc)
    outer = ExitStack()
    sb = lambda n, s, d: outer.enter_context(nc.sbuf_tensor(n, s, d))

    umask = sb("umask", [128, 128], F32)
    ident = sb("ident", [128, 128], BF16)
    ones128 = sb("ones128", [128, 128], BF16)
    onesk1 = sb("onesk1", [1, 128], BF16)
    selc = sb("selc", [128, 128], BF16)
    seleps = sb("seleps", [128, 128], BF16)
    seltc = sb("seltc", [16, NG * 128], BF16)
    rmask = sb("rmask", [128, 8], F32)
    bs1_sb = sb("bs1_sb", [128, 32], F32)
    bs2h_sb = sb("bs2h_sb", [128, 8], F32)
    bqkv_sb = sb("bqkv_sb", [128, 24], F32)
    bo_sb = sb("bo_sb", [128, 8], F32)
    gw_sb = sb("gw_sb", [128, 64], BF16)
    gwf_sb = sb("gwf_sb", [128, 64], F32)
    gwlo_sb = sb("gwlo_sb", [128, 64], BF16)
    gbb_sb = sb("gbb_sb", [128, 8], F32)
    ec5 = sb("ec5", [128, 1], F32)
    halfc = sb("halfc", [128, 1], F32)
    triexc = sb("triexc", [128, 128], F32)
    dcrow = sb("dcrow", [128, 32], F32)
    iota128 = sb("iota128", [128, 1], F32)
    iotaj = sb("iotaj", [8, NGRP], F32)
    onesfr = sb("onesfr", [1, 128], F32)
    onesfc = sb("onesfc", [128, 1], F32)
    onesf2 = sb("onesf2", [128, 2], F32)

    def load_consts():
        p.DS(umask[:, :], umask_p)
        p.DG(ident[:, :], ident_p)
        p.DG(ones128[:, :], ones_p)
        p.DG(onesk1[:, :], ones_p[0:1, :])
        p.DG(selc[:, :], sel_p)
        p.DG(seleps[:, :], seleps_p)
        p.DG(seltc[:, :], selt_p)
        p.DS(rmask[:, :], rmask_p)
        p.DS(bs1_sb[:, :], bs1_p)
        p.DS(bqkv_sb[:, 0:8], bq_p)
        p.DS(bqkv_sb[:, 8:16], bk_p)
        p.DS(bqkv_sb[:, 16:24], bv_p)
        p.DS(bo_sb[:, :], bo_p)
        p.DG(gw_sb[:, :], gw_p)
        p.DS(gwf_sb[:, :], gw_p)
        p.DS(gbb_sb[:, :], gbb_p)
        p.DS(bs2h_sb[:, :], bs2_p)
        p.DS(triexc[:, :], triex_p)
        p.DS(dcrow[:, :], dcrow_p)
        p.DS(iota128[:, :], iota128_p)
        p.DS(iotaj[:, :], iotaj_p)
        p.wV(p.sds, p.dsc)
        p.wV(p.sdg, p.dgc)
        p.V("memset", ec5[:, :], EPS_LN)
        p.V("memset", halfc[:, :], 0.5)
        p.V("memset", onesfr[:, :], 1.0)
        p.V("memset", onesfc[:, :], 1.0)
        p.V("memset", onesf2[:, :], 1.0)
        p.V("tensor_scalar_mul", bs2h_sb[:, :], bs2h_sb[:, :], 0.5)
        p.V("tensor_sub", gwlo_sb[:, :], gwf_sb[:, :], gw_sb[:, :])

    cast = {"jobs": [], "idx": 0, "wr": [0, 0, 0, 0], "buf": None,
            "ws_mark": None}
    if "ffn1" in phases or "ffn2" in phases:
        jb = cast["jobs"]
        for dc in range(32):
            jb.append((wcs1[dc * 128:(dc + 1) * 128, :],
                       ws1_p[dc].rearrange("r k c -> r (k c)"), 128))
        for dm in range(8):
            w_ap = ws2_p[dm].rearrange("r k c -> r (k c)")
            for qq in range(4):
                jb.append((wcs2[dm * 128:(dm + 1) * 128, qq * 1024:(qq + 1) * 1024],
                           w_ap[:, qq * 1024:(qq + 1) * 1024], 128))
        jb.append((wcb2[0:8, :], b2n_p, 8))
        for e in range(NE):
            for dc in range(32):
                r0 = (e * 32 + dc) * 128
                jb.append((wc1[r0:r0 + 128, :],
                           w1_p[e, dc].rearrange("r k c -> r (k c)"), 128))
            for k in range(32):
                r0 = (e * 32 + k) * 128
                jb.append((wc2[r0:r0 + 128, :], w2t_p[e, k], 128))

    def cast_step(n):
        if cast["buf"] is None:
            return
        for _ in range(n):
            if not cast["jobs"]:
                return
            dst, src, nrows = cast["jobs"].pop(0)
            s = cast["idx"] % 4
            cast["idx"] += 1
            slot = cast["buf"][0:nrows, s * 1024:(s + 1) * 1024]
            if cast["wr"][s]:
                p.wG(p.sq(f"sc{s % 2}"), cast["wr"][s])
            rd = p.DQ(f"w{s}", slot, src)
            p.wG(p.sq(f"w{s}"), rd)
            cast["wr"][s] = p.DQ(f"sc{s % 2}", dst, slot)
            if cast["idx"] == 65 and cast["ws_mark"] is None:
                cast["ws_mark"] = (p.streams["sc0"][2], p.streams["sc1"][2])

    def cast_need_ws():
        while cast["jobs"] and cast["ws_mark"] is None:
            cast_step(8)
        if cast["ws_mark"] is not None:
            p.wG(p.sq("sc0"), cast["ws_mark"][0])
            p.wG(p.sq("sc1"), cast["ws_mark"][1])

    def cast_flush():
        while cast["jobs"]:
            cast_step(64)

    def layernorm(src, c0, xn_slice, xc2, cb, sqb, t1, mbc, rstd, ps_sx, ps_sq):
        last_mm = 0
        cast_done = [0, 0]
        for ch in range(8):
            t = xc2[ch % 2]
            if ch >= 2:
                p.wS(p.sa, cast_done[ch % 2])
            dv = p.DQ(f"x{ch % 2}", t[:, :], src[ch * 128:(ch + 1) * 128, c0:c0 + 1024])
            p.wA(p.sq(f"x{ch % 2}"), dv)
            if ch >= 1:
                p.wA(p.spe, last_mm)
            p.A("activation", cb[:, :], t[:, :], AF.Copy)
            cast_done[ch % 2] = p.A("activation", sqb[:, :], t[:, :], AF.Square)
            p.wT(p.sa, cast_done[ch % 2])
            for n in range(2):
                sl = slice(n * 512, (n + 1) * 512)
                p.MM(ps_sx[:, sl], ones128[:, :], cb[:, sl], start=(ch == 0),
                     stop=(ch == 7))
                last_mm = p.MM(ps_sq[:, sl], ones128[:, :], sqb[:, sl],
                               start=(ch == 0), stop=(ch == 7), inc=True)
        p.wV(p.spe, last_mm)
        p.V("tensor_scalar_mul", mbc[:, :], ps_sx, 1.0 / D)
        p.V("tensor_mul", rstd[:, :], mbc[:, :], mbc[:, :])
        p.V("scalar_tensor_tensor", rstd[:, :], ps_sq, 1.0 / D, rstd[:, :],
            ALU.mult, ALU.subtract)
        p.wA(p.sv, p.vc)
        sq_v = p.A("activation", rstd[:, :], rstd[:, :], AF.Sqrt, bias=ec5[:, :])
        p.wV(p.sa, sq_v)
        p.V("reciprocal", rstd[:, :], rstd[:, :])
        p.wS(p.sa, p.ac)
        for ch in range(8):
            t = xc2[ch % 2]
            dv = p.DQ(f"x{ch % 2}", t[:, :], src[ch * 128:(ch + 1) * 128, c0:c0 + 1024])
            p.wV(p.sq(f"x{ch % 2}"), dv)
            p.V("tensor_sub", t1[:, :], t[:, :], mbc[:, :])
            f32dst, bf16dst = xn_slice(ch)
            if f32dst is not None:
                p.V("tensor_mul", f32dst, t1[:, :], rstd[:, :])
                p.V("tensor_copy", bf16dst, f32dst)
                p.wS(p.sv, p.vc - 2)
            else:
                p.V("tensor_mul", bf16dst, t1[:, :], rstd[:, :])
                p.wS(p.sv, p.vc - 1)
        return p.vc

    def ffn_call(x_src, x_dst, tag):
        stc = ExitStack()
        csb = lambda n, s, d: stc.enter_context(nc.sbuf_tensor(f"{tag}{n}", s, d))
        cps = lambda n, s, d: stc.enter_context(nc.psum_tensor(f"{tag}{n}", s, d))
        eqall = csb("eqall", [128, 128], F32)
        indall = csb("indall", [128, 128], F32)
        posall = csb("posall", [128, 128], F32)
        comb12 = csb("comb12", [128, 32], F32)
        slotsI = csb("slotsI", [128, 32], I32)
        iidxI = csb("iidxI", [128, NGRP * 32], I32)
        bidxI = csb("bidxI", [128, NGRP], I32)
        ownerI = csb("ownerI", [2, NGRP], I32)
        sbbc = csb("sbbc", [128, 8], F32)
        cumbase = csb("cumbase", [1, 8], F32)
        crow = csb("crow", [1, 8], F32)
        gce = csb("gce", [1, 8], F32)
        gbrow = csb("gbrow", [1, 8], F32)
        gbcol = csb("gbcol", [8, 1], F32)
        cmp8 = csb("cmp8", [8, NGRP], F32)
        ownF2 = csb("ownF2", [2, NGRP], F32)
        own2 = csb("own2", [1, NGRP], F32)
        wbaseF = csb("wbaseF", [128, NGRP], F32)
        bF = csb("bF", [128, NGRP], F32)
        iidxF = csb("iidxF", [128, 32], F32)
        xn_tm = [csb(f"xtm{i}", [128, 1024], BF16) for i in range(2)]
        psA = cps("psA", [128, 2048], F32)
        psB = cps("psB", [128, 1024], F32)
        psC = cps("psC", [128, 1024], F32)
        if cast["jobs"] and cast["buf"] is None:
            cast["buf"] = csb("castbuf", [128, 4096], BF16)

        sth = ExitStack()
        hsb = lambda n, s, d: sth.enter_context(nc.sbuf_tensor(f"{tag}h{n}", s, d))
        xn = hsb("xn", [128, 8 * 1024], BF16)
        xnf = hsb("xnf", [128, 8 * 1024], F32)
        hsh = hsb("hsh", [128, 32 * 1024], BF16)
        ybf = hsb("ybf", [128, 8 * 1024], BF16)
        w1b = [hsb(f"w1b{i}", [128, 1024], BF16) for i in range(2)]
        w2b = [hsb(f"w2b{i}", [128, 4096], BF16) for i in range(2)]
        xc2 = [hsb(f"xc{i}", [128, 1024], F32) for i in range(2)]
        cbt = hsb("cbt", [128, 1024], BF16)
        sqbt = hsb("sqbt", [128, 1024], BF16)
        t1 = hsb("t1", [128, 1024], F32)
        mbc = hsb("mbc", [128, 1024], F32)
        rstd = hsb("rstd", [128, 1024], F32)
        g8 = [hsb(f"g8_{i}", [128, 8], F32) for i in range(5)]
        gcl = [hsb(f"gc_{i}", [128, 1], F32) for i in range(3)]
        xnlo = hsh[:, 0:8 * 1024]
        xo2 = [xnf[:, 0:1024], xnf[:, 1024:2048]]

        p.V("memset", cumbase[:, :], 0.0)

        sh_mm1_grp = [0] * 32
        sh_mm1_evac = [0] * 32
        sh_mm2_grp = [0] * 8
        sh_ybf_evac = [0] * 8

        def phaseA(half):
            c0 = half * 1024
            lnv = layernorm(x_src, c0,
                            lambda ch: (xnf[:, ch * 1024:(ch + 1) * 1024],
                                        xn[:, ch * 1024:(ch + 1) * 1024]),
                            xc2, cbt, sqbt, t1, mbc, rstd,
                            psA[:, 0:1024], psA[:, 1024:2048])
            p.wS(p.sv, lnv)
            for ch in range(8):
                p.DQ(f"o{ch % 2}", xnstage[ch * 128:(ch + 1) * 128, c0:c0 + 1024],
                     xn[:, ch * 1024:(ch + 1) * 1024])
                cast_step(2)
            for ch in range(8):
                p.V("tensor_sub", xnlo[:, ch * 1024:(ch + 1) * 1024],
                    xnf[:, ch * 1024:(ch + 1) * 1024], xn[:, ch * 1024:(ch + 1) * 1024])
            p.wT(p.sv, p.vc)

        def phaseB(half):
            lg_evac = 0
            for c in range(8):
                q = half * 8 + c
                q8 = slice(q * 8, (q + 1) * 8)
                if lg_evac:
                    p.wT(p.sv, lg_evac)
                for k in range(8):
                    ksl = slice(k * 1024 + c * 128, k * 1024 + (c + 1) * 128)
                    gsl = slice(k * 8, (k + 1) * 8)
                    p.MM(psC[:, 0:8], xn[:, ksl], gw_sb[:, gsl], start=(k == 0),
                         stop=False)
                    p.MM(psC[:, 0:8], xnlo[:, ksl], gw_sb[:, gsl], start=False,
                         stop=False)
                    gmm = p.MM(psC[:, 0:8], xn[:, ksl], gwlo_sb[:, gsl], start=False,
                               stop=(k == 7), inc=(k == 7))
                p.wV(p.spe, gmm)
                lg = g8[0]
                p.V("tensor_copy", lg[:, :], psC[:, 0:8])
                lg_evac = p.V("tensor_add", lg[:, :], lg[:, :], gbb_sb[:, :])
                p.V("reduce_max", gcl[0][:, :], lg[:, :], AX)
                p.V("tensor_single_scalar", eqall[:, q8], lg[:, :], gcl[0][:, :],
                    ALU.is_ge)
                p.V("scalar_tensor_tensor", g8[2][:, :], eqall[:, q8], -1e30, lg[:, :],
                    ALU.mult, ALU.add)
                p.V("reduce_max", gcl[1][:, :], g8[2][:, :], AX)
                dv_ = p.V("tensor_single_scalar", g8[3][:, :], lg[:, :], gcl[0][:, :],
                          ALU.subtract)
                p.wA(p.sv, dv_)
                ev = p.A("activation", g8[3][:, :], g8[3][:, :], AF.Exp)
                m21 = p.V("tensor_single_scalar", gcl[2][:, :], gcl[1][:, :],
                          gcl[0][:, :], ALU.subtract)
                p.wA(p.sv, m21)
                e2v = p.A("activation", gcl[2][:, :], gcl[2][:, :], AF.Exp)
                p.wV(p.sa, e2v)
                p.V("tensor_scalar_add", gcl[2][:, :], gcl[2][:, :], 1.0)
                p.V("reciprocal", gcl[2][:, :], gcl[2][:, :])
                p.V("tensor_single_scalar", indall[:, q8], lg[:, :], gcl[1][:, :],
                    ALU.is_ge)
                p.wV(p.sa, ev)
                p.V("tensor_mul", g8[4][:, :], indall[:, q8], g8[3][:, :])
                p.V("tensor_single_scalar", g8[4][:, :], g8[4][:, :], gcl[2][:, :],
                    ALU.mult)
                p.V("tensor_scalar_mul", g8[4][:, :], g8[4][:, :], 0.5)
                p.V("tensor_mul", g8[2][:, :], eqall[:, q8], g8[4][:, :])
                p.V("reduce_sum", comb12[:, half * 16 + c:half * 16 + c + 1],
                    g8[2][:, :], AX)
                p.V("reduce_sum", gcl[0][:, :], g8[4][:, :], AX)
                p.V("tensor_sub", comb12[:, half * 16 + 8 + c:half * 16 + 9 + c],
                    gcl[0][:, :], comb12[:, half * 16 + c:half * 16 + c + 1])
                p.wT(p.sv, p.vc)
                p.MM(psC[:, 512:520], triexc[:, :], indall[:, q8], start=True,
                     stop=False)
                p.MM(psC[:, 512:520], onesfr[:, :], cumbase[:, :], start=False,
                     stop=True, inc=True)
                tmm = p.MM(psB[0:1, 0:8], onesfc[:, :], indall[:, q8], start=True,
                           stop=True, inc=True)
                p.wV(p.spe, tmm)
                p.V("tensor_copy", posall[:, q8], psC[:, 512:520])
                p.V("tensor_add", cumbase[:, :], cumbase[:, :], psB[0:1, 0:8])
                cast_step(3)

        def phaseE():
            p.V("memset", gce[:, :], 0.0)
            for k in range(11):
                p.V("tensor_single_scalar", crow[:, :], cumbase[:, :],
                    float(k * GS + 1), ALU.is_ge)
                p.V("tensor_add", gce[:, :], gce[:, :], crow[:, :])
            p.V("tensor_tensor_scan", gbrow[:, :], gce[:, :], gce[:, :], 0.0,
                ALU.add, ALU.bypass)
            sv1 = p.V("tensor_sub", gbrow[:, :], gbrow[:, :], gce[:, :])
            p.wT(p.sv, sv1)
            m1 = p.MM(psB[0:8, 8:9], gbrow[:, :], onesfr[0:1, 0:1], start=True,
                      stop=True, inc=True)
            p.wV(p.spe, m1)
            p.V("tensor_copy", gbcol[:, :], psB[0:8, 8:9])
            cv = p.V("tensor_single_scalar", cmp8[:, :], iotaj[:, :], gbcol[:, :],
                     ALU.is_ge)
            p.wT(p.sv, cv)
            m2 = p.MM(psC[0:2, 0:NGRP], onesf2[0:8, 0:2], cmp8[:, :], start=True,
                      stop=True, inc=True)
            p.wV(p.spe, m2)
            p.V("tensor_scalar_add", ownF2[:, :], psC[0:2, 0:NGRP], -1.0)
            p.V("tensor_copy", ownerI[:, :], ownF2[:, :])
            o1 = p.V("tensor_scalar_mul", own2[:, :], ownF2[0:1, :], 4096.0)
            p.wT(p.sv, o1)
            m3 = p.MM(psC[:, 512:512 + NGRP], onesfr[:, :], own2[:, :], start=True,
                      stop=True, inc=True)
            p.wV(p.spe, m3)
            p.V("tensor_single_scalar", wbaseF[:, :], psC[:, 512:512 + NGRP],
                iota128[:, :], ALU.add)
            o2 = p.V("tensor_scalar_mul", own2[:, :], ownF2[0:1, :], 128.0)
            p.wT(p.sv, o2)
            m4 = p.MM(psB[:, 512:512 + NGRP], onesfr[:, :], own2[:, :], start=True,
                      stop=True, inc=True)
            p.wV(p.spe, m4)
            p.V("tensor_single_scalar", bF[:, :], psB[:, 512:512 + NGRP],
                iota128[:, :], ALU.add)
            p.V("tensor_copy", bidxI[:, :], bF[:, :])
            o3 = p.V("tensor_scalar_mul", gbrow[:, :], gbrow[:, :], float(GS))
            p.wT(p.sv, o3)
            m5 = p.MM(psB[:, 544:552], onesfr[:, :], gbrow[:, :], start=True,
                      stop=True, inc=True)
            p.wV(p.spe, m5)
            p.V("tensor_copy", sbbc[:, :], psB[:, 544:552])
            for j in range(NGRP):
                p.V("tensor_single_scalar", iidxF[:, :], dcrow[:, :],
                    wbaseF[:, j:j + 1], ALU.add)
                p.V("tensor_copy", iidxI[:, j * 32:(j + 1) * 32], iidxF[:, :])
            p.wS(p.sq("o0"), p.streams["o0"][2])
            p.wS(p.sq("o1"), p.streams["o1"][2])
            scg = [0, 0]
            for q in range(16):
                par = q % 2
                if scg[par]:
                    p.wS(p.sq(f"sc{par}"), scg[par])
                dvt = p.DT(f"t{par}", xn_tm[par][:, :],
                           xnstage[0:1024, q * 128:(q + 1) * 128])
                p.V("tensor_add", g8[0][:, :], posall[:, q * 8:(q + 1) * 8],
                    sbbc[:, :])
                p.V("tensor_mul", g8[1][:, :], eqall[:, q * 8:(q + 1) * 8],
                    g8[0][:, :])
                p.V("reduce_sum", gcl[0][:, :], g8[1][:, :], AX)
                p.V("tensor_copy", slotsI[:, q:q + 1], gcl[0][:, :])
                p.V("tensor_mul", g8[1][:, :], indall[:, q * 8:(q + 1) * 8],
                    g8[0][:, :])
                p.V("reduce_sum", gcl[1][:, :], g8[1][:, :], AX)
                p.V("tensor_sub", gcl[1][:, :], gcl[1][:, :], gcl[0][:, :])
                slv = p.V("tensor_copy", slotsI[:, 16 + q:17 + q], gcl[1][:, :])
                p.wG(p.sv, slv)
                p.wG(p.sq(f"t{par}"), dvt)
                p.DQS(f"sc{par}", estage, slotsI[:, q:q + 1], xn_tm[par][:, :])
                scg[par] = p.DQS(f"sc{par}", estage, slotsI[:, 16 + q:17 + q],
                                 xn_tm[par][:, :])
                cast_step(3)

        def phaseC(half):
            for dc in range(32):
                buf = w1b[dc % 2]
                gi = dc - 2 if dc >= 2 else 30 + dc
                if sh_mm1_grp[gi]:
                    p.wG(p.spe, sh_mm1_grp[gi])
                dv = p.DQ(f"w{dc % 2}", buf[:, :], wcs1[dc * 128:(dc + 1) * 128, :])
                p.wT(p.sq(f"w{dc % 2}"), dv)
                pr = psA[:, (dc % 2) * 1024:(dc % 2 + 1) * 1024]
                if sh_mm1_evac[gi]:
                    p.wT(p.sa, sh_mm1_evac[gi])
                if dc == 0:
                    p.wT(p.sv, p.vc)
                for k in range(8):
                    p.MM(pr[:, 0:512], buf[:, k * 128:(k + 1) * 128],
                         xn[:, k * 1024:k * 1024 + 512], start=(k == 0), stop=(k == 7))
                    p.MM(pr[:, 512:1024], buf[:, k * 128:(k + 1) * 128],
                         xn[:, k * 1024 + 512:(k + 1) * 1024], start=(k == 0),
                         stop=(k == 7), inc=(k == 7))
                sh_mm1_grp[dc] = p.pec
                p.wA(p.spe, sh_mm1_grp[dc])
                sh_mm1_evac[dc] = p.A("activation", hsh[:, dc * 1024:(dc + 1) * 1024],
                                      pr, AF.Gelu, bias=bs1_sb[:, dc:dc + 1])
                cast_step(4)
            for dm in range(8):
                buf = w2b[dm % 2]
                gi = dm - 2 if dm >= 2 else 6 + dm
                if sh_mm2_grp[gi]:
                    p.wG(p.spe, sh_mm2_grp[gi])
                dv = p.DQ(f"w{2 + dm % 2}", buf[:, :], wcs2[dm * 128:(dm + 1) * 128, :])
                p.wT(p.sq(f"w{2 + dm % 2}"), dv)
                pr = psB[:, 0:1024] if dm % 2 == 0 else psA[:, 0:1024]
                if dm == 0 or dm % 2 == 1:
                    p.wT(p.sa, sh_mm1_evac[31])
                for k in range(32):
                    p.MM(pr[:, 0:512], buf[:, k * 128:(k + 1) * 128],
                         hsh[:, k * 1024:k * 1024 + 512], start=(k == 0), stop=(k == 31))
                    p.MM(pr[:, 512:1024], buf[:, k * 128:(k + 1) * 128],
                         hsh[:, k * 1024 + 512:(k + 1) * 1024], start=(k == 0),
                         stop=(k == 31), inc=(k == 31))
                sh_mm2_grp[dm] = p.pec
                p.wA(p.spe, sh_mm2_grp[dm])
                sh_ybf_evac[dm] = p.A("activation", ybf[:, dm * 1024:(dm + 1) * 1024],
                                      pr, AF.Identity, bias=bs2h_sb[:, dm:dm + 1],
                                      scale=halfc[:, :])
                cast_step(4)

        def phaseD(half):
            c0 = half * 1024
            uvh = [0] * 8
            xout_dma = [0] * 8
            for ch in range(8):
                t = xc2[ch % 2]
                o = xo2[ch % 2]
                if ch >= 2:
                    p.wS(p.sq(f"o{ch % 2}"), xout_dma[ch - 2])
                    p.wS(p.sv, uvh[ch - 2])
                dv = p.DQ(f"x{ch % 2}", t[:, :],
                          x_src[ch * 128:(ch + 1) * 128, c0:c0 + 1024])
                p.wV(p.sq(f"x{ch % 2}"), dv)
                if ch >= 2:
                    p.wV(p.sq(f"o{ch % 2}"), xout_dma[ch - 2])
                p.wV(p.sa, sh_ybf_evac[ch])
                uvh[ch] = uv = p.V("tensor_add", o, t[:, :],
                                   ybf[:, ch * 1024:(ch + 1) * 1024])
                p.wS(p.sv, uv)
                xout_dma[ch] = p.DQ(f"o{ch % 2}",
                                    xmid[ch * 128:(ch + 1) * 128, c0:c0 + 1024], o)
                cast_step(3)
            for par in range(2):
                p.wV(p.sq(f"o{par}"), p.streams[f"o{par}"][2])
            return p.vc

        def phaseFG():
            cast_flush()
            p.wG(p.sq("sc0"), p.streams["sc0"][2])
            p.wG(p.sq("sc1"), p.streams["sc1"][2])
            p.wS(p.sq("sc0"), p.streams["sc0"][2])
            p.wS(p.sq("sc1"), p.streams["sc1"][2])
            p.wS(p.sq("o0"), p.streams["o0"][2])
            p.wS(p.sq("o1"), p.streams["o1"][2])
            stf = ExitStack()
            fsb = lambda n, s, d: stf.enter_context(nc.sbuf_tensor(f"{tag}f{n}", s, d))
            he = [fsb(f"he{i}", [128, 32 * GS], BF16) for i in range(2)]
            xe = [fsb(f"xe{i}", [128, 8 * GS], BF16) for i in range(2)]
            w1g = [fsb(f"w1g{i}", [128, 1024], BF16) for i in range(2)]
            w2g = [fsb(f"w2g{i}", [128, 1024], BF16) for i in range(2)]
            ytm = [fsb(f"ytm{i}", [128, 1024], BF16) for i in range(2)]
            b1blk = [fsb(f"b1b{i}", [128, 32], F32) for i in range(2)]
            b2g = [fsb(f"b2g{i}", [2, 1024], BF16) for i in range(2)]
            g1 = [fsb(f"gg1{i}", [128, 1024], BF16) for i in range(2)]
            g2 = [fsb(f"gg2{i}", [128, 1024], BF16) for i in range(2)]
            gt1 = [fsb(f"gt1{i}", [128, 1024], F32) for i in range(2)]
            gt2 = [fsb(f"gt2{i}", [128, 1024], F32) for i in range(2)]
            ybc = [fsb(f"ybc{i}", [128, 1024], BF16) for i in range(2)]
            yf = [fsb(f"yf{i}", [128, 1024], BF16) for i in range(2)]
            xg = [fsb(f"xg{i}", [128, 1024], F32) for i in range(2)]
            xo = [fsb(f"xo{i}", [128, 1024], F32) for i in range(2)]
            psY = [psA[:, 1024:2048], psB[:, 0:1024], psC[:, 0:1024]]
            mm1_grp = [0] * 32
            mm1_evac = [0] * 32
            mm2k = [0] * 32
            mm1_last = [0] * NGRP
            mm1ev_last = [0] * NGRP
            mm2_last = [0] * NGRP
            xe_dv = [0, 0]
            b1_dv = [0, 0]
            b2_dv = [0, 0]
            psYev = [0, 0, 0]
            ytm_dma = [0, 0]
            for j in range(NGRP):
                par = j % 2
                if j >= 2:
                    p.wS(p.spe, mm1_last[j - 2])
                for dch in range(8):
                    xe_dv[par] = p.DT(
                        f"t{par}", xe[par][:, dch * GS:(dch + 1) * GS],
                        estage[j * GS:(j + 1) * GS, dch * 128:(dch + 1) * 128])
                if j >= 2:
                    p.wG(p.sa, mm1ev_last[j - 2])
                    p.wG(p.spe, mm2_last[j - 2])
                b1_dv[par] = p.DQI("w0", b1blk[par][:, :], b1r2_p, bidxI[:, j:j + 1])
                b2_dv[par] = p.DQI("w2", b2g[par][:, :], wcb2, ownerI[0:2, j:j + 1])
                if j >= 2:
                    p.wA(p.spe, mm2_last[j - 2])
                for dc in range(32):
                    buf = w1g[dc % 2]
                    gi = dc - 2 if dc >= 2 else 30 + dc
                    if mm1_grp[gi]:
                        p.wG(p.spe, mm1_grp[gi])
                    dv = p.DQI(f"w{dc % 2}", buf[:, :], wc1,
                               iidxI[:, j * 32 + dc:j * 32 + dc + 1])
                    p.wT(p.sq(f"w{dc % 2}"), dv)
                    if dc == 0:
                        p.wT(p.sq(f"t{par}"), xe_dv[par])
                        p.wT(p.sv, p.vc)
                    pr = psA[:, 0:GS] if dc % 2 == 0 else psA[:, 512:512 + GS]
                    if mm1_evac[gi]:
                        p.wT(p.sa, mm1_evac[gi])
                    for k in range(8):
                        p.MM(pr, buf[:, k * 128:(k + 1) * 128],
                             xe[par][:, k * GS:(k + 1) * GS], start=(k == 0),
                             stop=(k == 7), inc=(k == 7))
                    mm1_grp[dc] = p.pec
                    p.wA(p.spe, mm1_grp[dc])
                    if dc == 0:
                        p.wA(p.sq("w0"), b1_dv[par])
                    mm1_evac[dc] = p.A("activation",
                                       he[par][:, dc * GS:(dc + 1) * GS], pr,
                                       AF.Gelu, bias=b1blk[par][:, dc:dc + 1])
                mm1_last[j] = p.pec
                mm1ev_last[j] = mm1_evac[31]
                for k in range(32):
                    wbuf = w2g[k % 2]
                    gi = k - 2 if k >= 2 else 30 + k
                    if mm2k[gi]:
                        p.wG(p.spe, mm2k[gi])
                    dv = p.DQI(f"w{2 + k % 2}", wbuf[:, :], wc2,
                               iidxI[:, j * 32 + k:j * 32 + k + 1])
                    p.wT(p.sq(f"w{2 + k % 2}"), dv)
                    if k == 0:
                        p.wT(p.sa, mm1_evac[31])
                        p.wT(p.sv, psYev[2])
                    for sc in range(3):
                        for nn in range(2):
                            p.MM(psY[sc][:, nn * 512:(nn + 1) * 512],
                                 he[par][:, k * GS + sc * 128:k * GS + (sc + 1) * 128],
                                 wbuf[:, nn * 512:(nn + 1) * 512],
                                 start=(k == 0), stop=False,
                                 inc=(sc == 2 and nn == 1))
                    mm2k[k] = p.pec
                p.wT(p.sq("w2"), b2_dv[par])
                for sc in range(3):
                    for nn in range(2):
                        p.MM(psY[sc][:, nn * 512:(nn + 1) * 512], onesk1[:, :],
                             b2g[par][0:1, nn * 512:(nn + 1) * 512], start=False,
                             stop=True, inc=(sc == 2 and nn == 1))
                mm2_last[j] = p.pec
                p.wV(p.spe, mm2_last[j])
                for sc in range(3):
                    tp2 = sc % 2
                    if ytm_dma[tp2]:
                        p.wV(p.sq(f"sc{tp2}"), ytm_dma[tp2])
                    psYev[sc] = p.V("tensor_copy", ytm[tp2][:, :], psY[sc])
                    p.wG(p.sv, psYev[sc])
                    ytm_dma[tp2] = p.DQ(
                        f"sc{tp2}",
                        ystage[j * GS + sc * 128:j * GS + (sc + 1) * 128, :],
                        ytm[tp2][:, :])
            p.wG(p.sq("sc0"), p.streams["sc0"][2])
            p.wG(p.sq("sc1"), p.streams["sc1"][2])
            comb_v = [0, 0]
            ybc_dma = [0, 0]
            for q in range(16):
                par = q % 2
                if comb_v[par]:
                    p.wG(p.sv, comb_v[par])
                p.DQI(f"sc{par}", g1[par][:, :], ystage, slotsI[:, q:q + 1])
                dv2 = p.DQI(f"sc{par}", g2[par][:, :], ystage,
                            slotsI[:, 16 + q:17 + q])
                p.wV(p.sq(f"sc{par}"), dv2)
                hf, c = q // 8, q % 8
                p.V("tensor_single_scalar", gt1[par][:, :], g1[par][:, :],
                    comb12[:, hf * 16 + c:hf * 16 + c + 1], ALU.mult)
                p.V("tensor_single_scalar", gt2[par][:, :], g2[par][:, :],
                    comb12[:, hf * 16 + 8 + c:hf * 16 + 9 + c], ALU.mult)
                if ybc_dma[par]:
                    p.wV(p.sq(f"o{par}"), ybc_dma[par])
                comb_v[par] = p.V("tensor_add", ybc[par][:, :], gt1[par][:, :],
                                  gt2[par][:, :])
                p.wS(p.sv, comb_v[par])
                ybc_dma[par] = p.DQ(f"o{par}", ybuf[q * 128:(q + 1) * 128, :],
                                    ybc[par][:, :])
            p.wS(p.sq("o0"), p.streams["o0"][2])
            p.wS(p.sq("o1"), p.streams["o1"][2])
            addv = [0, 0]
            xodma = [0, 0]
            for i in range(16):
                ch, hf = i % 8, i // 8
                par = i % 2
                if addv[par]:
                    p.wS(p.sv, addv[par])
                dvy = p.DT(f"t{par}", yf[par][:, :],
                           ybuf[hf * 1024:(hf + 1) * 1024, ch * 128:(ch + 1) * 128])
                dvx = p.DQ(f"x{par}", xg[par][:, :],
                           xmid[ch * 128:(ch + 1) * 128, hf * 1024:(hf + 1) * 1024])
                p.wV(p.sq(f"t{par}"), dvy)
                p.wV(p.sq(f"x{par}"), dvx)
                if xodma[par]:
                    p.wV(p.sq(f"o{par}"), xodma[par])
                addv[par] = p.V("tensor_add", xo[par][:, :], xg[par][:, :],
                                yf[par][:, :])
                p.wS(p.sv, addv[par])
                xodma[par] = p.DQ(f"o{par}",
                                  x_dst[ch * 128:(ch + 1) * 128,
                                        hf * 1024:(hf + 1) * 1024], xo[par][:, :])
            for par in range(2):
                p.wV(p.sq(f"o{par}"), p.streams[f"o{par}"][2])
            stf.close()

        dvlast = [0]
        for half in range(2):
            if half == 1:
                p.wT(p.sa, p.ac)
                p.wT(p.sv, p.vc)
                p.wS(p.sv, p.vc)
                p.wV(p.sa, p.ac)
            phaseA(half)
            phaseB(half)
            if half == 1:
                phaseE()
            cast_need_ws()
            if dvlast[0]:
                p.wA(p.sv, dvlast[0])
            phaseC(half)
            dvlast[0] = phaseD(half)
        p.wT(p.sa, p.ac)
        p.wT(p.sv, p.vc)
        p.wA(p.sv, p.vc)
        p.wV(p.sa, p.ac)
        sth.close()
        phaseFG()
        if cast["buf"] is not None and not cast["jobs"]:
            cast["buf"] = None
        stc.close()

    dbg = dt("dbg", [128, 52224], F32, kind="ExternalOutput").ap() if attn_stop else None

    def attention(x_src, x_dst):
        st = ExitStack()
        asb = lambda n, s, d: st.enter_context(nc.sbuf_tensor("at" + n, s, d))
        outu = asb("outu", [128, NG * T], BF16)
        q_fm = asb("qfm", [128, NG * T], BF16)
        k_fm = asb("kfm", [128, NG * T], BF16)
        v_fm = asb("vfm", [128, NG * T], BF16)
        tp1 = asb("tp1", [16, T], F32)
        u1 = asb("u1", [128, T], F32)
        tp3 = u1[0:16, :]
        rs1b = asb("rs1b", [16, T], BF16)
        sab = asb("sab", [16, T], BF16)
        scb = asb("scb", [16, T], BF16)
        gb_ = asb("gb_", [16, T], BF16)

        s1 = ExitStack()
        s1sb = lambda n, s, d: s1.enter_context(nc.sbuf_tensor("a1" + n, s, d))
        s1ps = lambda n, s, d: s1.enter_context(nc.psum_tensor("a1" + n, s, d))
        n16 = s1sb("n16", [16, T], F32)
        rn16 = s1sb("rn16", [16, T], F32)
        ckscr = s1sb("cks", [128, T], F32)
        wprod = s1sb("wpr", [128, T], BF16)
        sib = scb
        sob = gb_
        wvb_t = s1sb("wvb", [128, 8 * 128], BF16)
        wvb = [wvb_t, wvb_t]
        gath = s1sb("gath", [128, 16], F32)
        pref1 = s1sb("pf1", [128, 16], F32)
        pref2 = s1sb("pf2", [128, 16], F32)
        pub = s1sb("pub", [128, 16], F32)
        te_t = s1sb("te", [16, 1], F32)
        ga16 = s1sb("ga16", [16, 1], F32)
        epref = s1sb("epf", [16, 1], F32)
        xc2 = [s1sb(f"xc{i}", [128, 1024], F32) for i in range(2)]
        cbt = wprod[:, 0:1024]
        sqbt = wprod[:, 1024:2048]
        t1 = ckscr[:, 0:1024]
        mbc = u1[:, 0:1024]
        rstd = u1[:, 1024:2048]
        atA = s1ps("atA", [128, 2048], F32)
        atS = s1ps("atS", [16, 2048], F32)

        p.DS(n16[:, :], n16_p)
        p.DS(rn16[:, :], rn16_p)
        p.wV(p.sds, p.dsc)

        def chk(name, dumps):
            if attn_stop != name:
                return False
            p.wG(p.sv, p.vc)
            p.wG(p.sa, p.ac)
            p.wG(p.spe, p.pec)
            col = 0
            for ap, w in dumps:
                rows = ap.shape[0]
                nc.gpsimd.dma_start(out=dbg[0:rows, col:col + w], in_=ap,
                                    allow_slow_non_contiguous=True).then_inc(p.sdg, 16)
                p.dgc += 16
                col += w
            p.wG(p.sdg, p.dgc)
            return True

        for half in range(2):
            c0 = half * 1024
            if half == 1:
                p.wT(p.sv, p.vc)
                p.wS(p.sv, p.vc)
            layernorm(x_src, c0,
                      lambda ch: (None, outu[:, ch * T + c0:ch * T + c0 + 1024]),
                      xc2, cbt, sqbt, t1, mbc, rstd,
                      atA[:, 0:1024], atA[:, 1024:2048])
        p.wT(p.sv, p.vc)

        proj_grp = [0, 0]
        pair_evac = [0, 0]
        for pi, (wp, dest, func, bofs) in enumerate(
                ((wq_p, q_fm, AF.Sigmoid, 0), (wk_p, k_fm, AF.Sigmoid, 8),
                 (wv_p, v_fm, AF.Identity, 16))):
            for g in range(NG):
                idx = pi * NG + g
                buf = wvb[idx % 2]
                if idx >= 1:
                    p.wG(p.spe, max(proj_grp))
                dv = p.DQ("w0", buf[:, :], wp[g].rearrange("r k c -> r (k c)"))
                p.wT(p.sq("w0"), dv)
                for t in range(4):
                    pr = atA[:, (t % 2) * 512:(t % 2) * 512 + 512]
                    p.wT(p.sa, pair_evac[t % 2])
                    for k in range(8):
                        p.MM(pr, buf[:, k * 128:(k + 1) * 128],
                             outu[:, k * T + t * 512:k * T + (t + 1) * 512],
                             start=(k == 0), stop=(k == 7), inc=(k == 7))
                    proj_grp[idx % 2] = p.pec
                    p.wA(p.spe, p.pec)
                    pair_evac[t % 2] = p.A(
                        "activation", dest[:, g * T + t * 512:g * T + (t + 1) * 512],
                        pr, func, bias=bqkv_sb[:, bofs + g:bofs + g + 1])
        p.wV(p.sa, p.ac)
        if chk("proj", [(q_fm[:, :], NG * T), (k_fm[:, :], NG * T),
                        (v_fm[:, :], NG * T)]):
            s1.close(); st.close(); return

        for g in range(NG):
            p.V("tensor_reduce", pub[:, g:g + 1], k_fm[:, g * T:(g + 1) * T], AX, ALU.add)
            p.V("tensor_reduce", pub[:, 8 + g:8 + g + 1], q_fm[:, g * T:(g + 1) * T], AX,
                ALU.add)

        def exchange128(pub_ap, cci, cco, pref_t):
            p.wG(p.sv, p.vc)
            dv = p.DG(cci.ap(), pub_ap)
            p.wG(p.sdg, dv)
            nc.gpsimd.collective_compute(
                "AllGather", ALU.bypass, replica_groups=[list(range(NC))],
                ins=[cci.ap().opt()], outs=[cco.ap().opt()]).then_inc(p.scc, 1)
            p.ccc += 1
            p.wG(p.scc, p.ccc)
            p.V("memset", pref_t[:, :], 0.0)
            for r in range(NC):
                p.wG(p.sv, p.vc)
                dv = p.DG(gath[:, :], cco.ap()[r * 128:(r + 1) * 128, :])
                p.wV(p.sdg, dv)
                p.V("scalar_tensor_tensor", pref_t[:, :], gath[:, :],
                    rmask[:, r:r + 1], pref_t[:, :], ALU.mult, ALU.add)
            return p.vc

        exchange128(pub[:, :], cc_in[0], cc_out[0], pref1)

        def scan_w_sel(data_fm, other_fm, pref_col, g, first, last):
            p.V("tensor_tensor_scan", ckscr[:, :], data_fm,
                data_fm[:, 0:1].broadcast_to((128, T)), pref_col, ALU.add, ALU.bypass)
            wv = p.V("scalar_tensor_tensor", wprod[:, :], other_fm, EPS_AT,
                     ckscr[:, :], ALU.add, ALU.mult)
            p.wT(p.sv, wv)
            for t in range(4):
                p.MM(atS[0:16, t * 512:(t + 1) * 512], selc[:, g * 16:(g + 1) * 16],
                     wprod[:, t * 512:(t + 1) * 512], start=(first and t < 4), stop=False)
                p.MM(atS[0:16, t * 512:(t + 1) * 512], seleps[:, g * 16:(g + 1) * 16],
                     other_fm[:, t * 512:(t + 1) * 512], start=False,
                     stop=last, inc=(t == 3))
            return p.pec

        for g in range(NG):
            if g > 0:
                p.wV(p.spe, p.pec)
            scan_w_sel(k_fm[:, g * T:(g + 1) * T], q_fm[:, g * T:(g + 1) * T],
                       pref1[:, g:g + 1], g, g == 0, g == 7)
        p.wV(p.spe, p.pec)
        p.V("tensor_scalar_add", tp3[:, :], atS[0:16, :], 64.0 * EPS_AT * EPS_AT)
        p.V("reciprocal", tp1[:, :], tp3[:, :])
        p.V("tensor_copy", rs1b[:, :], tp1[:, :])
        sib_v = p.V("tensor_mul", sib[:, :], n16[:, :], tp1[:, :])
        p.wT(p.sv, p.vc)
        for g in range(NG):
            if g > 0:
                p.wV(p.spe, p.pec)
            scan_w_sel(q_fm[:, g * T:(g + 1) * T], k_fm[:, g * T:(g + 1) * T],
                       pref1[:, 8 + g:8 + g + 1], g, g == 0, g == 7)
        p.wV(p.spe, p.pec)
        p.V("tensor_scalar_add", tp3[:, :], atS[0:16, :], 64.0 * EPS_AT * EPS_AT)
        p.V("reciprocal", tp1[:, :], tp3[:, :])
        sob_v = p.V("tensor_mul", sob[:, :], n16[:, :], tp1[:, :])
        if chk("s12", [(sib[0:16, :], T), (sob[0:16, :], T), (pref1[:, :], 16),
                       (pub[:, :], 16)]):
            s1.close(); st.close(); return

        def bc16(src_b16, g, ps):
            for t in range(4):
                p.MM(ps[:, t * 512:(t + 1) * 512], seltc[:, g * 128:(g + 1) * 128],
                     src_b16[:, t * 512:(t + 1) * 512], start=True, stop=True,
                     inc=(t == 3))
            return p.pec

        p.wT(p.sv, sob_v)
        for g in range(NG):
            mmv = bc16(sob, g, atA)
            p.wV(p.spe, mmv)
            kso_v = p.V("scalar_tensor_tensor", wprod[:, :], k_fm[:, g * T:(g + 1) * T],
                        1.0, atA[:, :], ALU.mult, ALU.mult, accum_out=pub[:, g:g + 1])
            p.wT(p.sv, kso_v)
            mmv = bc16(sib, g, atA)
            p.wV(p.spe, mmv)
            qsi_v = p.V("scalar_tensor_tensor", wprod[:, :], q_fm[:, g * T:(g + 1) * T],
                        1.0, atA[:, :], ALU.mult, ALU.mult, accum_out=pub[:, 8 + g:8 + g + 1])
            p.wT(p.sv, qsi_v)

        exchange128(pub[:, :], cc_in[1], cc_out[1], pref2)

        def c34_pass(bcast_b, base_fm_getter, dot_fm_getter, pref_ofs):
            for g in range(NG):
                if g > 0:
                    p.wV(p.spe, p.pec)
                mmv = bc16(bcast_b, g, atA)
                p.wV(p.spe, mmv)
                p.V("scalar_tensor_tensor", wprod[:, :], base_fm_getter(g), 1.0,
                    atA[:, :], ALU.mult, ALU.mult)
                p.V("tensor_tensor_scan", ckscr[:, :], wprod[:, :],
                    wprod[:, 0:1].broadcast_to((128, T)),
                    pref2[:, pref_ofs + g:pref_ofs + g + 1], ALU.add, ALU.bypass)
                wv = p.V("scalar_tensor_tensor", wprod[:, :], dot_fm_getter(g), EPS_AT,
                         ckscr[:, :], ALU.add, ALU.mult)
                p.wT(p.sv, wv)
                dot_fm = dot_fm_getter(g)
                for t in range(4):
                    p.MM(atS[0:16, t * 512:(t + 1) * 512], selc[:, g * 16:(g + 1) * 16],
                         wprod[:, t * 512:(t + 1) * 512], start=(g == 0), stop=False)
                    p.MM(atS[0:16, t * 512:(t + 1) * 512], seleps[:, g * 16:(g + 1) * 16],
                         dot_fm[:, t * 512:(t + 1) * 512], start=False,
                         stop=(g == 7), inc=(t == 3))
            return p.pec

        kg = lambda g: k_fm[:, g * T:(g + 1) * T]
        qg = lambda g: q_fm[:, g * T:(g + 1) * T]
        c34_pass(sob, kg, qg, 0)
        p.wV(p.spe, p.pec)
        p.V("tensor_scalar_add", tp3[:, :], atS[0:16, :], 64.0 * EPS_AT * EPS_AT)
        csk = p.V("tensor_mul", tp1[:, :], tp3[:, :], rn16[:, :])
        p.wA(p.sv, csk)
        sa_v = p.A("activation", sab[:, :], tp1[:, :], AF.Sigmoid)
        p.wT(p.sv, csk)
        c34_pass(sib, qg, kg, 8)
        p.wV(p.spe, p.pec)
        p.wV(p.sa, sa_v)
        p.V("tensor_scalar_add", tp3[:, :], atS[0:16, :], 64.0 * EPS_AT * EPS_AT)
        css = p.V("tensor_mul", tp1[:, :], tp3[:, :], rn16[:, :])
        p.wA(p.sv, css)
        e_v = p.A("activation", tp1[:, :], tp1[:, :], AF.Exp)
        p.wV(p.sa, e_v)
        if chk("c34", [(sab[0:16, :], T), (tp1[0:16, :], T), (pref2[:, :], 16)]):
            s1.close(); st.close(); return

        tev = p.V("tensor_reduce", te_t[:, :], tp1[:, :], AX, ALU.add)
        p.wG(p.sv, tev)
        dv = p.DG(cc3_in.ap(), te_t[:, :])
        p.wG(p.sdg, dv)
        nc.gpsimd.collective_compute(
            "AllGather", ALU.bypass, replica_groups=[list(range(NC))],
            ins=[cc3_in.ap().opt()], outs=[cc3_out.ap().opt()]).then_inc(p.scc, 1)
        p.ccc += 1
        p.wG(p.scc, p.ccc)
        p.V("memset", epref[:, :], 0.0)
        for r in range(NC):
            p.wG(p.sv, p.vc)
            dv = p.DG(ga16[:, :], cc3_out.ap()[r * 16:(r + 1) * 16, :])
            p.wV(p.sdg, dv)
            p.V("scalar_tensor_tensor", epref[:, :], ga16[:, :], rmask[0:16, r:r + 1],
                epref[:, :], ALU.mult, ALU.add)

        p.V("tensor_tensor_scan", tp3[:, :], tp1[:, :],
            tp1[:, 0:1].broadcast_to((16, T)), epref[:, :], ALU.add, ALU.bypass)
        p.V("reciprocal", tp3[:, :], tp3[:, :])
        p.V("tensor_mul", tp3[:, :], tp1[:, :], tp3[:, :])
        p.V("tensor_mul", scb[:, :], tp3[:, :], n16[:, :])
        gv = p.V("tensor_mul", gb_[:, :], sab[:, :], rs1b[:, :])
        p.wT(p.sv, gv)
        if chk("sc", [(scb[0:16, :], T), (gb_[0:16, :], T), (epref[0:16, :], 1)]):
            s1.close(); st.close(); return
        s1.close()

        s2 = ExitStack()
        s2sb = lambda n, s, d: s2.enter_context(nc.sbuf_tensor("a2" + n, s, d))
        s2ps = lambda n, s, d: s2.enter_context(nc.psum_tensor("a2" + n, s, d))
        gbc = s2sb("gbc", [128, T], BF16)
        pub4 = s2sb("pb4", [128, NG * 64], F32)
        sgth = s2sb("sgt", [128, NG * 64], F32)
        spref_b = s2sb("spb", [128, NG * 64], BF16)
        ktm_all = s2sb("ktmA", [128, 16 * 128], BF16)
        vtm_all = s2sb("vtmA", [128, 16 * 128], BF16)
        amu_all = s2sb("amuA", [128, 16 * 256], BF16)
        snap = s2sb("snp", [128, 128], BF16)
        s_acc = s2sb("sacc", [128, 128], F32)
        wosb = [s2sb(f"wo{i}", [128, 8 * 128], BF16) for i in range(2)]
        zt5 = s2sb("zt5", [128, 512], F32)
        e1t = [zt5[:, 0:256], zt5[:, 256:512]]
        xc5 = [s2sb(f"xc5{i}", [128, 512], F32) for i in range(2)]
        xo5 = [s2sb(f"xo5{i}", [128, 512], F32) for i in range(2)]
        pBig = s2ps("pB", [128, 2048], F32)
        pS = s2ps("pS", [128, 128], F32)
        pT = s2ps("pT", [128, 2048], BF16)
        pW = s2ps("pW", [128, 512], F32)

        tr_evac = [0, 0]
        amu_evac = [0, 0]
        out_evac = [0, 0]
        supd = 0
        j_ng, j_nc, j_steps = NG, NCH, 5
        if attn_stop and attn_stop.startswith("j:"):
            parts = attn_stop.split(":")
            j_ng, j_nc = int(parts[1]), int(parts[2])
            if len(parts) > 3:
                j_steps = int(parts[3])
        for g in range(j_ng):
            gofs = g * T
            p.wT(p.sv, p.vc)
            mmv = bc16(scb, g, pBig)
            p.wV(p.spe, mmv)
            vp = p.V("scalar_tensor_tensor", v_fm[:, gofs:gofs + T],
                     v_fm[:, gofs:gofs + T], 1.0, pBig[:, :], ALU.mult, ALU.mult)
            p.wT(p.sv, vp)
            mmv = bc16(gb_, g, pBig)
            p.wV(p.spe, mmv)
            gvv = p.V("tensor_copy", gbc[:, :], pBig[:, :])
            p.V("memset", s_acc[:, :], 0.0)
            p.V("memset", snap[:, :], 0.0)
            p.wT(p.sv, p.vc)
            for c in range(j_nc):
                if j_nc == 0:
                    break
                pa = c % 2
                cs = slice(gofs + c * 128, gofs + (c + 1) * 128)
                tslot = pT[:, pa * 1024:pa * 1024 + 256]
                p.wT(p.sv, tr_evac[pa])
                p.TR(tslot[:, 0:128], k_fm[:, cs], ident[:, :])
                trv = p.TR(tslot[:, 128:256], v_fm[:, cs], ident[:, :])
                p.wV(p.spe, trv)
                p.V("tensor_copy", ktm_all[:, c * 128:(c + 1) * 128], tslot[:, 0:128])
                tr_evac[pa] = p.V("tensor_copy", vtm_all[:, c * 128:(c + 1) * 128],
                                  tslot[:, 128:256])
                if j_steps < 2:
                    continue
                pgA = pBig[:, pa * 1024:pa * 1024 + 128]
                pgB = pBig[:, pa * 1024 + 512:pa * 1024 + 640]
                p.wT(p.sv, amu_evac[pa])
                p.MM(pgA, k_fm[0:64, cs], q_fm[0:64, cs], start=True, stop=True,
                     tp=(0, 0))
                gmv = p.MM(pgB, k_fm[64:128, cs], q_fm[64:128, cs],
                           start=True, stop=True, tp=(64, 0), inc=True)
                p.wV(p.spe, gmv)
                p.V("tensor_mul", amu_all[:, c * 256:c * 256 + 128], pgA, umask[:, :])
                amu_evac[pa] = p.V("tensor_mul", amu_all[:, c * 256 + 128:c * 256 + 256],
                                   pgB, umask[:, :])
            if j_steps < 3:
                continue
            p.wT(p.sv, p.vc)
            sdone = 0
            for c in range(j_nc):
                cs = slice(gofs + c * 128, gofs + (c + 1) * 128)
                poi = pBig[:, 1024:1152]
                pon = pBig[:, 1536:1664]
                if c > 0:
                    p.wT(p.sv, sdone)
                supd = p.MM(pS[:, :], ktm_all[:, c * 128:(c + 1) * 128],
                            vtm_all[:, c * 128:(c + 1) * 128], start=True,
                            stop=True, inc=True)
                p.wT(p.sv, out_evac[0])
                p.MM(poi[0:64, :], vtm_all[:, c * 128:c * 128 + 64],
                     amu_all[:, c * 256:c * 256 + 128],
                     start=True, stop=True, tp=(0, 0))
                omv = p.MM(poi[64:128, :], vtm_all[:, c * 128 + 64:(c + 1) * 128],
                           amu_all[:, c * 256 + 128:c * 256 + 256],
                           start=True, stop=True, tp=(0, 64), inc=True)
                if c > 0:
                    p.MM(pon[0:64, :], snap[0:64, 0:64], q_fm[0:64, cs],
                         start=True, stop=True, tp=(0, 0))
                    omv = p.MM(pon[64:128, :], snap[64:128, 64:128], q_fm[64:128, cs],
                               start=True, stop=True, tp=(64, 64), inc=True)
                p.wV(p.spe, supd)
                p.V("tensor_add", s_acc[:, :], s_acc[:, :], pS[:, :])
                sdone = p.V("tensor_copy", snap[:, :], s_acc[:, :])
                p.wV(p.spe, omv)
                if c > 0:
                    p.V("tensor_mul", e1t[0][:, 0:128], poi,
                        gbc[:, c * 128:(c + 1) * 128])
                    p.V("tensor_mul", e1t[0][:, 128:256], pon,
                        gbc[:, c * 128:(c + 1) * 128])
                    out_evac[0] = p.V("tensor_add", outu[:, cs], e1t[0][:, 0:128],
                                      e1t[0][:, 128:256])
                else:
                    out_evac[0] = p.V("tensor_mul", outu[:, cs], poi,
                                      gbc[:, 0:128])
            p.V("tensor_copy", pub4[0:64, g * 64:(g + 1) * 64], s_acc[0:64, 0:64])
            sv_ = p.V("tensor_copy", pub4[64:128, g * 64:(g + 1) * 64],
                      s_acc[64:128, 64:128])
            p.wT(p.sv, sv_)

        if chk("j", [(outu[:, :], NG * T), (pub4[:, :], NG * 64)]) or \
           (attn_stop and attn_stop.startswith("j:") and
            chk(attn_stop, [(outu[:, :], NG * T), (pub4[:, :], NG * 64)])):
            s2.close(); st.close(); return
        p.wG(p.sv, p.vc)
        dv = p.DG(cc4_in.ap(), pub4[:, :])
        p.wG(p.sdg, dv)
        nc.gpsimd.collective_compute(
            "AllGather", ALU.bypass, replica_groups=[list(range(NC))],
            ins=[cc4_in.ap().opt()], outs=[cc4_out.ap().opt()]).then_inc(p.scc, 1)
        p.ccc += 1
        p.wG(p.scc, p.ccc)
        p.wV(p.scc, p.ccc)
        p.V("memset", pub4[:, :], 0.0)
        for r in range(NC):
            p.wG(p.sv, p.vc)
            dv = p.DG(sgth[:, :], cc4_out.ap()[r * 128:(r + 1) * 128, :])
            p.wV(p.sdg, dv)
            p.V("scalar_tensor_tensor", pub4[:, :], sgth[:, :], rmask[:, r:r + 1],
                pub4[:, :], ALU.mult, ALU.add)
        spv = p.V("tensor_copy", spref_b[:, :], pub4[:, :])

        for g in range(NG):
            gofs = g * T
            p.wT(p.sv, p.vc)
            mmv = bc16(gb_, g, pBig)
            p.wV(p.spe, mmv)
            gvv = p.V("tensor_copy", gbc[:, :], pBig[:, :])
            p.wT(p.sv, gvv)
            p.wT(p.sv, spv)
            for t in range(4):
                pr = pW[:, :] if t % 2 == 0 else pBig[:, 0:512]
                ts = slice(gofs + t * 512, gofs + (t + 1) * 512)
                p.MM(pr[0:64, :], spref_b[0:64, g * 64:(g + 1) * 64],
                     q_fm[0:64, ts], start=True, stop=True, tp=(0, 0))
                lmv = p.MM(pr[64:128, :], spref_b[64:128, g * 64:(g + 1) * 64],
                           q_fm[64:128, ts], start=True, stop=True, tp=(64, 64),
                           inc=True)
                p.wV(p.spe, lmv)
                p.V("tensor_mul", zt5[:, :], pr, gbc[:, t * 512:(t + 1) * 512])
                ov = p.V("tensor_add", outu[:, ts], outu[:, ts], zt5[:, :])
                p.wT(p.sv, ov)

        if chk("l", [(outu[:, :], NG * T)]):
            s2.close(); st.close(); return
        if attn_stop == "xres":
            p.wG(p.sv, p.vc)
            for ch in range(8):
                p.DG(dbg[:, ch * 512:(ch + 1) * 512], x_src[ch * 128:(ch + 1) * 128, 0:512])
            p.wG(p.sdg, p.dgc)
            s2.close(); st.close(); return
        wo_grp = [0, 0]
        xo_dma = [0] * 32
        uv_hist = [0] * 32
        prev_z_read = p.vc
        p.wA(p.sv, p.vc)
        p.wT(p.sv, p.vc)
        for dm in range(8):
            buf = wosb[dm % 2]
            if dm >= 2:
                p.wG(p.spe, wo_grp[dm % 2])
            dv = p.DQ(f"w{dm % 2}", buf[:, :], wo_p[dm].rearrange("r k c -> r (k c)"))
            p.wT(p.sq(f"w{dm % 2}"), dv)
            for t in range(4):
                n = dm * 4 + t
                pr = pW[:, :] if t % 2 == 0 else pBig[:, 0:512]
                if n >= 2:
                    p.wT(p.sa, z_evac[n - 2])
                for g in range(NG):
                    p.MM(pr, buf[:, g * 128:(g + 1) * 128],
                         outu[:, g * T + t * 512:g * T + (t + 1) * 512],
                         start=(g == 0), stop=(g == 7), inc=(g == 7))
                wo_grp[dm % 2] = p.pec
                p.wA(p.spe, p.pec)
                p.wA(p.sv, prev_z_read)
                if n == 0:
                    z_evac = [0] * 32
                z_evac[n] = p.A("activation", zt5[:, :], pr, AF.Identity,
                                bias=bo_sb[:, dm:dm + 1])
                if attn_stop == "wo" and dm == 0 and t in (0, 1):
                    p.wG(p.sa, z_evac[n])
                    nc.gpsimd.dma_start(out=dbg[:, t * 512:(t + 1) * 512],
                                        in_=zt5[:, :]).then_inc(p.sdg, 16)
                    p.dgc += 16
                xt_ = xc5[n % 2]
                xo_ = xo5[n % 2]
                if n >= 2:
                    p.wG(p.sq(f"o{n % 2}"), xo_dma[n - 2])
                    p.wG(p.sv, uv_hist[n - 2])
                dvx = p.DQ(f"xg{n % 2}", xt_[:, :], x_src[dm * 128:(dm + 1) * 128,
                                                          t * 512:(t + 1) * 512])
                p.wV(p.sq(f"xg{n % 2}"), dvx)
                p.wV(p.sa, z_evac[n])
                if n >= 2:
                    p.wV(p.sq(f"o{n % 2}"), xo_dma[n - 2])
                prev_z_read = uv = p.V("tensor_add", xo_[:, :], xt_[:, :], zt5[:, :])
                uv_hist[n] = uv
                p.wS(p.sv, uv)
                xo_dma[n] = p.DQ(f"o{n % 2}",
                                 x_dst[dm * 128:(dm + 1) * 128, t * 512:(t + 1) * 512],
                                 xo_[:, :])
                if attn_stop == "n0" and n == 0:
                    p.wG(p.sv, uv)
                    p.DG(dbg[:, 0:512], xt_[:, :])
                    p.DG(dbg[:, 512:1024], zt5[:, :])
                    p.DG(dbg[:, 1024:1536], xo_[:, :])
                    p.wG(p.sdg, p.dgc)
        for par in range(2):
            p.wV(p.sq(f"o{par}"), p.streams[f"o{par}"][2])
        if attn_stop == "xres2":
            p.wG(p.sds, p.dsc)
            for ch in range(8):
                p.DG(dbg[:, ch * 512:(ch + 1) * 512], x_dst[ch * 128:(ch + 1) * 128, 0:512])
            p.wG(p.sdg, p.dgc)
        s2.close()
        st.close()

    load_consts()
    p.drain_barrier()
    if "ffn1" in phases:
        ffn_call(x_in, xres, "f1")
        p.drain_barrier()
    else:
        for ch in range(8):
            p.DS(xres[ch * 128:(ch + 1) * 128, :], x_in[ch * 128:(ch + 1) * 128, :])
        p.drain_barrier()
    if "attn" in phases:
        attention(xres, xres)
        p.drain_barrier()
    if "ffn2" in phases:
        ffn_call(xres, x_out, "f2")
    else:
        for ch in range(8):
            p.DS(x_out[ch * 128:(ch + 1) * 128, :], xres[ch * 128:(ch + 1) * 128, :])
    p.wS(p.sds, p.dsc)
    p.wG(p.sdg, p.dgc)
    for nm, (sem_, kind, cntv) in p.streams.items():
        if cntv:
            (p.wG if kind == "g" else p.wS)(sem_, cntv)
    p.ctx.close()
    outer.close()
    return nc



def _shared_inputs(inputs):
    f32c = lambda a: np.ascontiguousarray(np.asarray(a, dtype=np.float32))
    wq = f32c(inputs["L0_attn_Wq"]); wk = f32c(inputs["L0_attn_Wk"])
    wv = f32c(inputs["L0_attn_Wv"]); wo = f32c(inputs["L0_attn_Wo"])
    w1 = f32c(inputs["L0_ffn_w1"]); w2 = f32c(inputs["L0_ffn_w2"])
    ws1 = f32c(inputs["L0_ffn_ws1"]); ws2 = f32c(inputs["L0_ffn_ws2"])

    def blk(w, kc, dcc):
        return np.ascontiguousarray(
            w.reshape(kc, 128, dcc, 128).transpose(2, 1, 0, 3))

    d = {}
    d["wq_p"] = blk(wq, 8, 8)
    d["wk_p"] = blk(wk, 8, 8)
    d["wv_p"] = blk(wv, 8, 8)
    d["wo_p"] = blk(wo, 8, 8)
    d["w1_p"] = np.ascontiguousarray(
        w1.reshape(NE, 8, 128, 32, 128).transpose(0, 3, 2, 1, 4))
    d["w2t_p"] = np.ascontiguousarray(w2.reshape(NE, 32, 128, 1024))
    d["ws1_p"] = blk(ws1, 8, 32)
    d["ws2_p"] = blk(ws2, 32, 8)
    gw = f32c(inputs["L0_ffn_gate_w"])
    d["gw_p"] = np.ascontiguousarray(
        gw.reshape(8, 128, 8).transpose(1, 0, 2).reshape(128, 64))
    d["gbb_p"] = np.ascontiguousarray(
        np.broadcast_to(f32c(inputs["L0_ffn_gate_b"]).reshape(1, 8), (128, 8)))
    d["b1r2_p"] = np.ascontiguousarray(
        f32c(inputs["L0_ffn_b1"]).reshape(NE, 32, 128).transpose(0, 2, 1)
        .reshape(NE * 128, 32))
    d["b2n_p"] = f32c(inputs["L0_ffn_b2"]).reshape(NE, 1024)
    d["triex_p"] = np.triu(np.ones((128, 128), np.float32), 1)
    d["dcrow_p"] = np.ascontiguousarray(np.broadcast_to(
        (np.arange(32, dtype=np.float32) * 128.0)[None, :], (128, 32)))
    d["iota128_p"] = np.arange(128, dtype=np.float32).reshape(128, 1)
    d["iotaj_p"] = np.ascontiguousarray(np.broadcast_to(
        np.arange(NGRP, dtype=np.float32)[None, :], (8, NGRP)))
    d["bs1_p"] = np.ascontiguousarray(f32c(inputs["L0_ffn_bs1"]).reshape(32, 128).T)
    d["bs2_p"] = np.ascontiguousarray(f32c(inputs["L0_ffn_bs2"]).reshape(8, 128).T)
    d["bq_p"] = np.ascontiguousarray(f32c(inputs["L0_attn_bq"]).reshape(8, 128).T)
    d["bk_p"] = np.ascontiguousarray(f32c(inputs["L0_attn_bk"]).reshape(8, 128).T)
    d["bv_p"] = np.ascontiguousarray(f32c(inputs["L0_attn_bv"]).reshape(8, 128).T)
    d["bo_p"] = np.ascontiguousarray(f32c(inputs["L0_attn_bo"]).reshape(8, 128).T)
    d["umask_p"] = np.triu(np.ones((128, 128), np.float32))
    d["ident_p"] = np.eye(128, dtype=np.float32)
    d["ones_p"] = np.ones((128, 128), np.float32)
    sel = np.zeros((128, 128), np.float32)
    for g in range(NG):
        sel[0:64, g * 16 + 2 * g] = 1.0
        sel[64:128, g * 16 + 2 * g + 1] = 1.0
    d["sel_p"] = sel
    d["seleps_p"] = sel * 1e-6
    selt = np.zeros((16, NG * 128), np.float32)
    for g in range(NG):
        selt[2 * g, g * 128:g * 128 + 64] = 1.0
        selt[2 * g + 1, g * 128 + 64:g * 128 + 128] = 1.0
    d["selt_p"] = selt
    return d


def make_in_maps(inputs, nc_build=None):
    shared = _shared_inputs(inputs)
    x = np.asarray(inputs["L0_attn_x"], dtype=np.float32).reshape(NC * T, D)
    in_maps = []
    for i in range(NC):
        m = dict(shared)
        m["x_in"] = np.ascontiguousarray(x[i * T:(i + 1) * T, :].T)
        pos = (np.arange(T, dtype=np.float64) + 1.0 + i * T)
        m["n16_p"] = np.ascontiguousarray(
            np.broadcast_to(pos.astype(np.float32), (16, T)))
        m["rn16_p"] = np.ascontiguousarray(
            np.broadcast_to((1.0 / pos).astype(np.float32), (16, T)))
        rm = np.zeros((128, 8), np.float32)
        rm[:, :i] = 1.0
        m["rmask_p"] = rm
        in_maps.append(m)
    return in_maps


def kernel(**inputs):
    nc = build_program()
    in_maps = make_in_maps(inputs)
    res = run_bass_kernel_spmd(nc, in_maps, core_ids=list(range(NC)))
    out = np.concatenate([np.asarray(r["x_out"]).T for r in res.results], axis=0)
    return out.reshape(1, NC * T, D)
